# version 29
# speedup vs baseline: 1.1968x; 1.0053x over previous
import contextlib
import numpy as np
import ml_dtypes
import concourse.bass as bass
import concourse.mybir as mybir
from concourse.bass import IndirectOffsetOnAxis as IOA
from concourse.bass_utils import run_bass_kernel_spmd

F32 = mybir.dt.float32
BF16 = mybir.dt.bfloat16
I32 = mybir.dt.int32
AF = mybir.ActivationFunctionType
ALU = mybir.AluOpType
AX = mybir.AxisListType

B = 8
SEQ = 4096
D = 1024
H = 16
NCH = 8
CW = 512
DT = 8
NE = 8
CAP = 640
NSLOT = NE * CAP
NSLOT_PAD = NSLOT + 128
NPAIR = 4096
PB_ROWS = 4224
VTW = H * 65
LN_EPS = 1e-5

DEBUG = False
DEV_CHUNKS = NCH
DEV_FFN = True
DEV_NEXPERTS = NE
DEV_COMBINE = True
DEV_FFN_STAGE = 5
DEV_DUMP_ROUTING = False
_NC_CACHE = {}


class Prog:
    COMPUTE = ("pe", "dve", "act", "pool", "sp")
    BATCH_STREAMS = {"consts", "bias", "smload", "rin", "pfill", "wcast", "wcastf",
                     "pbfill", "sc_tok", "sc_pair", "sc_coef", "tload"}

    def __init__(self):
        self.ops = []
        self.stream_count = {}

    def op(self, eng, fn, reads=(), writes=(), dma=None, tag=""):
        rec = dict(eng=eng, fn=fn, reads=tuple(reads), writes=tuple(writes),
                   dma=dma, tag=tag, seq=len(self.ops))
        if dma is not None:
            idx = self.stream_count.get(dma, 0)
            self.stream_count[dma] = idx + 1
            rec["sidx"] = idx
        self.ops.append(rec)
        return rec

    def emit(self, nc, es):
        ops = self.ops
        eng_pos = {e: 0 for e in Prog.COMPUTE}
        for o in ops:
            o["epos"] = eng_pos[o["eng"]]
            eng_pos[o["eng"]] += 1

        reg = {}
        for o in ops:
            deps = []
            for r in o["reads"]:
                e = reg.get(r)
                if e and e["w"] is not None:
                    deps.append(e["w"])
            for w in o["writes"]:
                e = reg.get(w)
                if e:
                    if e["w"] is not None:
                        deps.append(e["w"])
                    deps.extend(e["r"])
            seen = set()
            o["deps"] = [d for d in deps
                         if not (id(d) in seen or seen.add(id(d)))]
            for r in o["reads"]:
                reg.setdefault(r, {"w": None, "r": []})["r"].append(o)
            for w in o["writes"]:
                reg[w] = {"w": o, "r": []}

        def okey(o):
            return ("s", o["dma"]) if o["dma"] is not None else o["eng"]

        def oidx(o):
            return o["sidx"] if o["dma"] is not None else o["epos"]

        eng_clock = {e: {} for e in Prog.COMPUTE}
        for o in ops:
            clk = eng_clock[o["eng"]]
            waits = {}
            for d in o["deps"]:
                if d["dma"] is None and o["dma"] is None and \
                   d["eng"] == "pe" and o["eng"] == "pe":
                    continue
                k, i = okey(d), oidx(d)
                if d["dma"] is not None and d["dma"] in Prog.BATCH_STREAMS:
                    i = self.stream_count[d["dma"]] - 1
                if clk.get(k, -1) >= i:
                    continue
                if waits.get(k, -1) < i:
                    waits[k] = i
            for d in o["deps"]:
                for k2, v2 in d["vc"].items():
                    if clk.get(k2, -1) < v2:
                        clk[k2] = v2
                k, i = okey(d), oidx(d)
                if d["dma"] is not None and d["dma"] in Prog.BATCH_STREAMS:
                    i = self.stream_count[d["dma"]] - 1
                if clk.get(k, -1) < i:
                    clk[k] = i
                if not (d["dma"] is None and o["dma"] is None
                        and d["eng"] == "pe" and o["eng"] == "pe"):
                    d["need_inc"] = True
            o["waits"] = waits
            vc = dict(clk)
            vc[okey(o)] = oidx(o)
            o["vc"] = vc

        inc_prefix = {}
        for e in Prog.COMPUTE:
            eops = sorted((o for o in ops if o["eng"] == e and o["dma"] is None),
                          key=lambda x: x["epos"])
            cum = 0
            mp = {}
            for o in eops:
                if o.get("need_inc"):
                    cum += 1
                mp[o["epos"]] = cum
            inc_prefix[e] = mp

        sems = {}
        for e in Prog.COMPUTE:
            sems[e] = es.enter_context(nc.semaphore(f"se_{e}"))
        for s in self.stream_count:
            sems[("s", s)] = es.enter_context(nc.semaphore(f"sd_{s}"))

        def wait_val(k, i):
            if isinstance(k, tuple):
                return 16 * (i + 1)
            return inc_prefix[k][i]

        def emit_for(ename, eng_obj):
            for o in ops:
                if o["eng"] != ename:
                    continue
                for k, i in sorted(o["waits"].items(), key=lambda kv: str(kv[0])):
                    v = wait_val(k, i)
                    if v > 0:
                        eng_obj.wait_ge(sems[k], v)
                instr = o["fn"](eng_obj)
                if o["dma"] is not None:
                    instr.then_inc(sems[("s", o["dma"])], 16)
                elif o.get("need_inc"):
                    instr.then_inc(sems[o["eng"]], 1)

        with nc.Block() as blk:
            @blk.tensor
            def _(e):
                emit_for("pe", e)

            @blk.vector
            def _(e):
                emit_for("dve", e)

            @blk.scalar
            def _(e):
                emit_for("act", e)

            @blk.gpsimd
            def _(e):
                emit_for("pool", e)

            @blk.sync
            def _(e):
                emit_for("sp", e)
        return sems


def host_constants():
    t = np.arange(D // 2, dtype=np.float64)
    freq = np.power(10000.0, -2.0 * t / D)
    pos = np.arange(SEQ, dtype=np.float64)
    ang = pos[None, :] * freq[:, None]
    return {
        "cos_t": np.cos(ang).astype(np.float32).astype(ml_dtypes.bfloat16),
        "sin_t": np.sin(ang).astype(np.float32).astype(ml_dtypes.bfloat16),
        "ident_f32": np.eye(128, dtype=np.float32),
        "ident_bf": np.eye(128, dtype=ml_dtypes.bfloat16),
        "trimask01": np.triu(np.ones((128, 128), np.float32)).astype(ml_dtypes.bfloat16),
        "tristrict": np.tril(np.ones((128, 128), np.float32), -1).T.copy().astype(ml_dtypes.bfloat16),
        "ones_row": np.ones((1, 512), dtype=ml_dtypes.bfloat16),
        "ones_col": np.ones((128, 1), dtype=ml_dtypes.bfloat16),
        "ones_f32": np.ones((1, 128), dtype=np.float32),
        "tokid": np.ascontiguousarray((np.arange(NPAIR, dtype=np.int32) // 2).reshape(128, 32)),
        "pairid": np.ascontiguousarray(np.arange(NPAIR, dtype=np.int32).reshape(128, 32)),
        "pf_invtok": np.zeros((NSLOT_PAD, 1), np.int32),
        "pf_invpair": np.full((NSLOT_PAD, 1), NPAIR, np.int32),
        "pf_coef": np.zeros((NSLOT_PAD, 1), np.float32),
        "zeros_bf": np.zeros((128, 1024), ml_dtypes.bfloat16),
        "epscol": np.full((128, 1), LN_EPS, np.float32),
    }


CONST_SPECS = {
    "cos_t": ([D // 2, SEQ], BF16), "sin_t": ([D // 2, SEQ], BF16),
    "ident_f32": ([128, 128], F32), "ident_bf": ([128, 128], BF16),
    "trimask01": ([128, 128], BF16), "tristrict": ([128, 128], BF16),
    "ones_row": ([1, 512], BF16), "ones_col": ([128, 1], BF16),
    "ones_f32": ([1, 128], F32),
    "tokid": ([128, 32], I32), "pairid": ([128, 32], I32),
    "pf_invtok": ([NSLOT_PAD, 1], I32), "pf_invpair": ([NSLOT_PAD, 1], I32),
    "pf_coef": ([NSLOT_PAD, 1], F32),
    "zeros_bf": ([128, 1024], BF16),
    "epscol": ([128, 1], F32),
}

INPUT_SPECS = {
    "x": ([SEQ, D], F32),
    "wq_e": ([D, D // 2], F32), "wq_o": ([D, D // 2], F32),
    "wk_e": ([D, D // 2], F32), "wk_o": ([D, D // 2], F32),
    "wv": ([D, D], F32), "wo": ([D, D], F32),
    "bq_e": ([128, 4], F32), "bq_o": ([128, 4], F32),
    "bk_e": ([128, 4], F32), "bk_o": ([128, 4], F32),
    "bv": ([1, D], F32), "bo": ([128, 8], F32),
    "gug": ([NE * D, D], F32), "guu": ([NE * D, D], F32),
    "dn": ([NE * D, D], F32),
    "gub_g": ([128, 64], F32), "gub_u": ([128, 64], F32), "dnb": ([128, 64], F32),
    "ridx_pair": ([128, 32], I32),
    "rw_bp": ([128, 256], F32),
    "maskcol": ([128, 1], F32),
}


def build_program(debug=False):
    nc = bass.Bass("TRN2", target_bir_lowering=False)
    es = contextlib.ExitStack()
    p = Prog()

    io = {}
    for name, (shape, dt) in {**INPUT_SPECS, **CONST_SPECS}.items():
        io[name] = nc.dram_tensor(name, shape, dt, kind="ExternalInput").ap()
    out_d = nc.dram_tensor("out", [SEQ, D], F32, kind="ExternalOutput").ap()
    dbg = {}

    def dbg_t(name, shape, dt=F32):
        dbg[name] = nc.dram_tensor("dbg_" + name, shape, dt, kind="ExternalOutput").ap()
        return dbg[name]

    w_bf = {}
    for wn in ("wq_e", "wq_o", "wk_e", "wk_o", "wv", "wo", "gug", "guu", "dn"):
        w_bf[wn] = nc.dram_tensor("wbf_" + wn, INPUT_SPECS[wn][0], BF16, kind="Internal").ap()
    h1nat_d = nc.dram_tensor("h1nat_d", [SEQ // 2, D], F32, kind="Internal").ap()
    xhi_d = nc.dram_tensor("xhi_d", [SEQ, D], BF16, kind="Internal").ap()
    xlo_d = nc.dram_tensor("xlo_d", [SEQ, D], BF16, kind="Internal").ap()
    h1bf_d = nc.dram_tensor("h1bf_d", [SEQ // 2, D], BF16, kind="Internal").ap()
    xg_d = nc.dram_tensor("xg_d", [2 * CAP, D], BF16, kind="Internal").ap()
    yt_d = nc.dram_tensor("yt_d", [2 * D, CAP], BF16, kind="Internal").ap()
    invtok_d = nc.dram_tensor("invtok_d", [NSLOT_PAD, 1], I32, kind="Internal").ap()
    invpair_d = nc.dram_tensor("invpair_d", [NSLOT_PAD, 1], I32, kind="Internal").ap()
    coefslot_d = nc.dram_tensor("coefslot_d", [NSLOT_PAD, 1], F32, kind="Internal").ap()
    pairbuf_d = nc.dram_tensor("pairbuf_d", [PB_ROWS, D], BF16, kind="Internal").ap()

    sb = lambda name, shape, dt: es.enter_context(nc.sbuf_tensor(name, shape, dt))

    KT = sb("KT", [128, DT * SEQ], BF16)
    V = sb("V", [128, 32 * VTW], BF16)
    wbuf = sb("wbuf", [128, 8192], BF16)
    xnat = sb("xnat", [128, 1024], F32)
    xT32 = sb("xT32", [128, 4096], F32)
    xTbf = sb("xTbf", [128, 4096], BF16)
    eo = sb("eo", [128, 4096], BF16)
    rot = xTbf
    tab = sb("tab", [128, 4096], BF16)
    pT = sb("pT", [128, 1024], BF16)
    bcf = sb("bcf", [128, 1024], F32)
    smalls = sb("smalls", [128, 48], F32)
    gub_sb = sb("gub_sb", [128, 192], F32)
    rout = sb("rout", [128, 260], F32)
    moei = sb("moei", [128, 176], I32)
    coefc = sb("coefc", [128, 40], F32)
    cntbf = sb("cntbf", [128, 1], BF16)
    ident32 = sb("ident32", [128, 128], F32)
    identbf = sb("identbf", [128, 128], BF16)
    trimask = sb("trimask", [128, 128], BF16)
    tristr = sb("tristr", [128, 128], BF16)
    onesrow = sb("onesrow", [1, 512], BF16)
    onescol = sb("onescol", [128, 1], BF16)
    onesf32 = sb("onesf32", [1, 128], F32)
    bvrow = sb("bvrow", [1, 1024], BF16)
    statrow = sb("statrow", [1, 1024], F32)

    xeT_ab = [KT[:, 0:DT * CAP], V[:, 0:DT * CAP]]
    intT = KT[:, 5120:5120 + DT * CAP]
    yTbf = KT[:, 10240:10240 + DT * CAP]
    glu_all = yTbf
    pbt = KT[:, 15360:15360 + 2048]
    xgbf = KT[:, 17408:17408 + 1024]
    ynat = KT[:, 18432:18432 + 1024]
    ynat2 = KT[:, 29184:29184 + 1024]
    pbt2 = KT[:, 30208:30208 + 2048]
    u1s = KT[:, 19456:19456 + 512]
    sgs = KT[:, 19968:19968 + 512]
    g1s = KT[:, 20480:20480 + 512]
    wbuf2 = KT[:, 20992:20992 + 8192]
    xgp = [xT32[:, 0:1024], xT32[:, 1024:2048]]
    cmbp = [xT32[:, 2048:3072], xT32[:, 3072:4096]]

    scA = es.enter_context(nc.psum_tensor("scA", [128, 2048], F32))
    ctxA = es.enter_context(nc.psum_tensor("ctxA", [128, 512], F32))
    ctxB = es.enter_context(nc.psum_tensor("ctxB", [128, 512], F32))
    aux = es.enter_context(nc.psum_tensor("aux", [128, 512], F32))
    pbbf = es.enter_context(nc.psum_tensor("pbbf", [128, 1024], BF16))
    pb = [scA[:, 0:512], scA[:, 512:1024], scA[:, 1024:1536], scA[:, 1536:2048],
          ctxA[:, :], ctxB[:, :], aux[:, :]]

    def OP(eng, fn, r=(), w=(), dma=None, tag=""):
        return p.op(eng, fn, reads=r, writes=w, dma=dma, tag=tag)

    def load_set(name_id):
        def f(e):
            return e.add_instruction(mybir.InstLoadActFuncSet(
                name=e.bass.get_next_instruction_name(),
                act_func_set_id=name_id, ins=[], outs=[]))
        return f

    from concourse.hw_specs import get_activation_tables
    _tabnames = list(get_activation_tables(nc.m.arch).keys())
    SET_LNEXP = _tabnames.index("natural_log_exp_and_others")
    SET_SIG = _tabnames.index("sigmoid_and_others")
    OP("act", load_set(SET_LNEXP), tag="tab_lnexp0")

    for wn in ("wk_e", "wk_o", "wv", "wq_e", "wq_o", "wo"):
        OP("pool", lambda e, s=io[wn], d=w_bf[wn]: e.dma_start(d[:], s[:]),
           w=((wn + "_bf"),), dma=f"wc_{wn}", tag=f"cast_{wn}")
    for nm, dstt in (("ident_f32", ident32), ("ident_bf", identbf), ("trimask01", trimask),
                     ("tristrict", tristr), ("ones_row", onesrow), ("ones_col", onescol),
                     ("ones_f32", onesf32)):
        OP("sp", lambda e, s=io[nm], d=dstt: e.dma_start(d[:], s[:]),
           w=(nm,), dma="consts", tag=f"ld_{nm}")
    for i, nm in enumerate(("bq_e", "bq_o", "bk_e", "bk_o")):
        OP("sp", lambda e, s=io[nm], i=i: e.dma_start(smalls[:, 24 + 4 * i:28 + 4 * i], s[:]),
           w=(f"b_{nm}",), dma="smload", tag=f"ld_{nm}")
    OP("pool", lambda e: e.dma_start(bvrow[:], io["bv"][:]), w=("bvrow",), dma="bias")
    OP("sp", lambda e: e.dma_start(smalls[:, 0:8], io["bo"][:]), w=("bo",), dma="smload")
    OP("sp", lambda e: e.dma_start(smalls[:, 16:17], io["maskcol"][:]), w=("maskc",), dma="smload")
    OP("sp", lambda e: e.dma_start(smalls[:, 17:18], io["epscol"][:]), w=("epsc",), dma="smload")
    OP("sp", lambda e: e.dma_start(gub_sb[:, 0:64], io["gub_g"][:]), w=("gubg",), dma="smload")
    OP("sp", lambda e: e.dma_start(gub_sb[:, 64:128], io["gub_u"][:]), w=("gubu",), dma="smload")
    OP("sp", lambda e: e.dma_start(gub_sb[:, 128:192], io["dnb"][:]), w=("dnbc",), dma="smload")
    OP("sp", lambda e: e.dma_start(invtok_d[:], io["pf_invtok"][:]), w=("invtok_d",), dma="pfill")
    OP("sp", lambda e: e.dma_start(invpair_d[:], io["pf_invpair"][:]), w=("invpair_d",), dma="pfill")
    OP("sp", lambda e: e.dma_start(coefslot_d[:], io["pf_coef"][:]), w=("coefslot_d",), dma="pfill")
    for t in range(PB_ROWS // 128):
        OP("act", lambda e, t=t:
           e.dma_start(pairbuf_d[128 * t:128 * (t + 1), :], io["zeros_bf"][:]),
           w=(("pairbuf", t, 0), ("pairbuf", t, 1)), dma="pbfill", tag=f"pbz{t}")
    OP("sp", lambda e: e.dma_start(moei[:, 112:144], io["tokid"][:]), w=("tokid",), dma="rin")
    OP("sp", lambda e: e.dma_start(moei[:, 144:176], io["pairid"][:]), w=("pairid",), dma="rin")
    OP("sp", lambda e: e.dma_start(moei[:, 80:112], io["ridx_pair"][:]), w=("ridxi",), dma="rin")
    rwsb = sb("rwsb", [128, 256], F32)
    OP("sp", lambda e: e.dma_start(rwsb[:], io["rw_bp"][:]), w=("rwsb",), dma="rin")

    def routing():
        RF = lambda i: rout[:, 32 * i:32 * (i + 1)]
        OP("dve", lambda e: e.tensor_copy(RF(0), moei[:, 80:112]), r=("ridxi",), w=("r_ridx",))
        OP("dve", lambda e: e.memset(RF(5), 0.0), w=("r_slotp",))
        OP("dve", lambda e: e.memset(RF(7), 0.0), w=("r_coefp",))
        for ex in range(NE):
            OP("dve", lambda e, ex=ex: e.tensor_scalar(RF(1), RF(0), float(ex), None, ALU.is_equal),
               r=("r_ridx",), w=("r_eq",), tag=f"eq{ex}")
            OP("dve", lambda e: e.tensor_copy(RF(2), RF(1)), r=("r_eq",), w=("r_s0",))
            srcb, dstb, sreg, dreg = 2, 3, "r_s0", "r_s1"
            k = 1
            while k < 32:
                OP("dve", lambda e, s=srcb, d=dstb, k=k: e.tensor_copy(
                    rout[:, 32 * d:32 * d + k], rout[:, 32 * s:32 * s + k]),
                   r=(sreg,), w=(dreg,), tag=f"sc{ex}_{k}a")
                OP("dve", lambda e, s=srcb, d=dstb, k=k: e.tensor_tensor(
                    rout[:, 32 * d + k:32 * (d + 1)], rout[:, 32 * s + k:32 * (s + 1)],
                    rout[:, 32 * s:32 * (s + 1) - k], ALU.add),
                   r=(sreg,), w=(dreg,), tag=f"sc{ex}_{k}b")
                srcb, dstb, sreg, dreg = dstb, srcb, dreg, sreg
                k *= 2
            OP("dve", lambda e, s=srcb: e.tensor_tensor(RF(4), rout[:, 32 * s:32 * (s + 1)], RF(1),
               ALU.subtract), r=(sreg, "r_eq"), w=("r_excl",))
            OP("dve", lambda e: e.reduce_sum(rout[:, 256:257], RF(1), axis=AX.X),
               r=("r_eq",), w=("r_cnt",))
            OP("dve", lambda e: e.tensor_copy(cntbf[:], rout[:, 256:257]), r=("r_cnt",), w=("r_cntbf",))
            OP("pe", lambda e: e.matmul(pb[6][:, 0:1], tristr[:], cntbf[:], start=True, stop=True),
               r=("tristrict", "r_cntbf"), w=(("pb", 6),), tag=f"off{ex}")
            OP("dve", lambda e: e.tensor_copy(rout[:, 257:258], pb[6][:, 0:1]),
               r=(("pb", 6),), w=("r_offs",))
            OP("dve", lambda e: e.tensor_scalar(RF(4), RF(4), rout[:, 257:258], None, ALU.add),
               r=("r_excl", "r_offs"), w=("r_excl",), tag=f"rank{ex}")
            OP("dve", lambda e, ex=ex: e.tensor_scalar(RF(6), RF(4), float(CAP * ex), None, ALU.add),
               r=("r_excl",), w=("r_tmp",))
            OP("dve", lambda e: e.tensor_tensor(RF(6), RF(6), RF(1), ALU.mult),
               r=("r_tmp", "r_eq"), w=("r_tmp",))
            OP("dve", lambda e: e.tensor_tensor(RF(5), RF(5), RF(6), ALU.add),
               r=("r_slotp", "r_tmp"), w=("r_slotp",))
            OP("dve", lambda e, ex=ex: e.tensor_tensor(
                RF(6), rwsb[:].rearrange("p (g e) -> p g e", e=8)[:, :, ex], RF(1), ALU.mult),
               r=("rwsb", "r_eq"), w=("r_tmp",))
            OP("dve", lambda e: e.tensor_tensor(RF(7), RF(7), RF(6), ALU.add),
               r=("r_coefp", "r_tmp"), w=("r_coefp",))
        OP("dve", lambda e: e.tensor_scalar(RF(5), RF(5), float(NSLOT), None, ALU.min),
           r=("r_slotp",), w=("r_slotp",))
        OP("dve", lambda e: e.tensor_scalar(RF(7), RF(7), smalls[:, 16:17], None, ALU.mult),
           r=("r_coefp", "maskc"), w=("r_coefp",))
        OP("dve", lambda e: e.tensor_copy(moei[:, 80:112], RF(5)), r=("r_slotp",), w=("sloti",))
        for g in range(32):
            OP("pool", lambda e, g=g: e.indirect_dma_start(
                invtok_d[:], IOA(ap=moei[:, 80 + g:81 + g], axis=0), moei[:, 112 + g:113 + g], None),
               r=("sloti", "tokid", "invtok_d"), w=(("invtok_s", g),), dma="sc_tok", tag=f"stok{g}")
            OP("pool", lambda e, g=g: e.indirect_dma_start(
                invpair_d[:], IOA(ap=moei[:, 80 + g:81 + g], axis=0), moei[:, 144 + g:145 + g], None),
               r=("sloti", "pairid", "invpair_d"), w=(("invpair_s", g),), dma="sc_pair", tag=f"spair{g}")
            OP("pool", lambda e, g=g: e.indirect_dma_start(
                coefslot_d[:], IOA(ap=moei[:, 80 + g:81 + g], axis=0),
                rout[:, 32 * 7 + g:32 * 7 + g + 1], None),
               r=("sloti", "r_coefp", "coefslot_d"), w=(("coef_s", g),), dma="sc_coef", tag=f"scoef{g}")
        rd_tok = tuple(("invtok_s", g) for g in range(32)) + ("invtok_d",)
        rd_pair = tuple(("invpair_s", g) for g in range(32)) + ("invpair_d",)
        rd_coef = tuple(("coef_s", g) for g in range(32)) + ("coefslot_d",)
        for kg in range(40):
            OP("pool", lambda e, kg=kg: e.dma_start(moei[:, kg:kg + 1],
               invtok_d[128 * kg:128 * (kg + 1), :]),
               r=rd_tok, w=(("itc", kg),), dma="tload", tag=f"ltok{kg}")
            OP("pool", lambda e, kg=kg: e.dma_start(moei[:, 40 + kg:41 + kg],
               invpair_d[128 * kg:128 * (kg + 1), :]),
               r=rd_pair, w=(("ipc", kg),), dma="tload", tag=f"lpair{kg}")
            OP("pool", lambda e, kg=kg: e.dma_start(coefc[:, kg:kg + 1],
               coefslot_d[128 * kg:128 * (kg + 1), :]),
               r=rd_coef, w=(("cfc", kg),), dma="tload", tag=f"lcoef{kg}")

    if DEV_DUMP_ROUTING:
        for nm, dr, rr in (("invtok", invtok_d, tuple(("itc", kg) for kg in range(40))),
                           ("invpair", invpair_d, tuple(("ipc", kg) for kg in range(40))),
                           ("coefs", coefslot_d, tuple(("cfc", kg) for kg in range(40)))):
            dt_ = I32 if nm != "coefs" else F32
            dbg_t("r_" + nm, [NSLOT_PAD, 1], dt_)
            OP("sp", lambda e, nm=nm, dr=dr: e.dma_start(dbg["r_" + nm][:], dr[:]),
               r=rr, dma="dbg_" + nm, tag="dump_" + nm)

    for qt in range(32):
        par = qt % 2
        xbuf = xnat[:, :] if par == 0 else bcf[:, 0:1024]
        xreg = ("xnat", 0) if par == 0 else ("bcf", 0)
        eh = (("eo", 4 * par), ("eo", 4 * par + 1))
        el = (("eo", 4 * par + 2), ("eo", 4 * par + 3))
        OP("sp", lambda e, qt=qt, xbuf=xbuf: e.dma_start(xbuf, io["x"][128 * qt:128 * (qt + 1), :]),
           w=(xreg,), dma=f"xld{par}", tag=f"xl{qt}")
        OP("dve", lambda e, xbuf=xbuf, par=par: e.tensor_copy(eo[:, 2048 * par:2048 * par + 1024], xbuf),
           r=(xreg,), w=eh, tag=f"xh{qt}")
        OP("dve", lambda e, xbuf=xbuf, par=par: e.tensor_tensor(
           eo[:, 2048 * par + 1024:2048 * par + 2048], xbuf,
           eo[:, 2048 * par:2048 * par + 1024], ALU.subtract),
           r=(xreg,) + eh, w=el, tag=f"xl2{qt}")
        OP("pool", lambda e, qt=qt, par=par: e.dma_start(
           xhi_d[128 * qt:128 * (qt + 1), :], eo[:, 2048 * par:2048 * par + 1024]),
           r=eh, w=(("xhi", qt),), dma=f"xhist{par}", tag=f"xhs{qt}")
        OP("pool", lambda e, qt=qt, par=par: e.dma_start(
           xlo_d[128 * qt:128 * (qt + 1), :], eo[:, 2048 * par + 1024:2048 * par + 2048]),
           r=el, w=(("xlo", qt),), dma=f"xlost{par}", tag=f"xls{qt}")

    def load_w_pair(nm_e, nm_o, tag):
        OP("sp", lambda e: e.dma_start(
            wbuf[:, 0:4096].rearrange("p (t c) -> p t c", c=512),
            w_bf[nm_e][:, :].rearrange("(t p) c -> p t c", p=128)),
           r=((nm_e + "_bf"),), w=(("wbuf", 0),), dma="wld_e", tag=f"{tag}e")
        OP("sp", lambda e: e.dma_start(
            wbuf[:, 4096:8192].rearrange("p (t c) -> p t c", c=512),
            w_bf[nm_o][:, :].rearrange("(t p) c -> p t c", p=128)),
           r=((nm_o + "_bf"),), w=(("wbuf", 1),), dma="wld_o", tag=f"{tag}o")

    def load_w_full(nm, tag):
        OP("sp", lambda e: e.dma_start(
            wbuf[:, :].rearrange("p (t c) -> p t c", c=1024),
            w_bf[nm][:, :].rearrange("(t p) c -> p t c", p=128)),
           r=((nm + "_bf"),), w=(("wbuf", 0), ("wbuf", 1)), dma="wld_f", tag=tag)

    def s0_hi_tile(j, t):
        xhir = tuple(("xhi", 4 * j + q) for q in range(4))
        OP("sp", lambda e, t=t, j=j: e.dma_start(
           xTbf[:, CW * t:CW * (t + 1)],
           xhi_d[CW * j:CW * (j + 1), 128 * t:128 * (t + 1)], transpose=True),
           r=xhir, w=(("xTbf", t),), dma=f"xhiT{t}", tag=f"xht{j}_{t}")

    def s0_hi_to32(j):
        for t in range(DT):
            OP("dve", lambda e, t=t: e.tensor_copy(
               xT32[:, CW * t:CW * (t + 1)], xTbf[:, CW * t:CW * (t + 1)]),
               r=(("xTbf", t),), w=(("xT32", t),), tag=f"xc{j}_{t}")

    def s0_lo(j):
        xlor = tuple(("xlo", 4 * j + q) for q in range(4))
        for t in range(DT):
            OP("sp", lambda e, t=t, j=j: e.dma_start(
               pT[:, 512 * (t % 2):512 * (t % 2) + 512],
               xlo_d[CW * j:CW * (j + 1), 128 * t:128 * (t + 1)], transpose=True),
               r=xlor, w=(("pT", t % 2),), dma="xloT", tag=f"xlt{j}_{t}")
            OP("dve", lambda e, t=t: e.tensor_tensor(
               xT32[:, CW * t:CW * (t + 1)], xT32[:, CW * t:CW * (t + 1)],
               pT[:, 512 * (t % 2):512 * (t % 2) + 512], ALU.add),
               r=(("xT32", t), ("pT", 0), ("pT", 1)), w=(("xT32", t),), tag=f"xr{j}_{t}")

    def load_tables(j):
        for tt in range(4):
            OP("sp", lambda e, tt=tt, j=j: e.dma_start(
                tab[:, CW * tt:CW * (tt + 1)],
                io["cos_t"][128 * tt:128 * (tt + 1), CW * j:CW * (j + 1)]),
               w=(("tab", tt),), dma="tabld", tag=f"cos{j}_{tt}")
            OP("sp", lambda e, tt=tt, j=j: e.dma_start(
                tab[:, CW * (4 + tt):CW * (5 + tt)],
                io["sin_t"][128 * tt:128 * (tt + 1), CW * j:CW * (j + 1)]),
               w=(("tab", 4 + tt),), dma="tabld", tag=f"sin{j}_{tt}")

    def proj_eo(j, bias_base, tag):
        xrt = tuple(("xTbf", t) for t in range(DT))
        bregs = ("b_bq_e", "b_bq_o") if bias_base == 24 else ("b_bk_e", "b_bk_o")
        for half in range(2):
            xr = xrt + (("wbuf", half),)
            for tt in range(4):
                bk = 2 * half + tt % 2
                for t in range(DT):
                    OP("pe", lambda e, t=t, tt=tt, half=half, bk=bk:
                       e.matmul(pb[bk][:], wbuf[:, 4096 * half + 512 * t + 128 * tt:
                                                4096 * half + 512 * t + 128 * (tt + 1)],
                                xTbf[:, CW * t:CW * (t + 1)], start=(t == 0), stop=(t == DT - 1)),
                       r=xr, w=(("pb", bk),), tag=f"{tag}mm{half}{tt}_{t}")
                bcol = bias_base + 4 * half + tt
                if tt % 2 == 0:
                    OP("dve", lambda e, half=half, tt=tt, bk=bk, bcol=bcol:
                       e.tensor_scalar(eo[:, CW * (4 * half + tt):CW * (4 * half + tt + 1)], pb[bk][:],
                                       smalls[:, bcol:bcol + 1], None, ALU.add),
                       r=(("pb", bk), bregs[half]), w=(("eo", 4 * half + tt),), tag=f"{tag}ev{half}{tt}")
                else:
                    OP("act", lambda e, half=half, tt=tt, bk=bk, bcol=bcol:
                       e.activation(eo[:, CW * (4 * half + tt):CW * (4 * half + tt + 1)], pb[bk][:],
                                    AF.Identity, bias=smalls[:, bcol:bcol + 1]),
                       r=(("pb", bk), bregs[half]), w=(("eo", 4 * half + tt),), tag=f"{tag}ev{half}{tt}")

    def rope_to(j, dst_half_fn, dst_region_fn, tag):
        tabs = tuple(("tab", i) for i in range(8))
        eos = tuple(("eo", i) for i in range(8))
        wr_top = tuple(dst_region_fn(i) for i in range(4))
        wr_bot = tuple(dst_region_fn(4 + i) for i in range(4))
        pbs = (("pb", 0), ("pb", 1), ("pb", 2), ("pb", 3))
        r3 = lambda ap: ap.rearrange("p (d s) -> p d s", s=512)
        E = lambda: r3(eo[:, 0:2048])
        Ox = lambda: r3(eo[:, 2048:4096])
        COS = lambda: r3(tab[:, 0:2048])
        SIN = lambda: r3(tab[:, 2048:4096])
        TMP = lambda: r3(scA[:, 0:2048])
        OP("dve", lambda e: e.tensor_tensor(dst_half_fn(0), E(), COS(), ALU.mult),
           r=eos[:4] + tabs, w=wr_top, tag=f"{tag}top1")
        OP("dve", lambda e: e.tensor_tensor(dst_half_fn(1), E(), SIN(), ALU.mult),
           r=eos[:4] + tabs, w=wr_bot, tag=f"{tag}bot1")
        OP("dve", lambda e: e.tensor_tensor(TMP(), Ox(), SIN(), ALU.mult),
           r=eos[4:] + tabs + pbs, w=pbs, tag=f"{tag}t2")
        OP("dve", lambda e: e.tensor_tensor(dst_half_fn(0), dst_half_fn(0), TMP(), ALU.subtract),
           r=wr_top + pbs, w=wr_top, tag=f"{tag}top2")
        OP("dve", lambda e: e.tensor_tensor(TMP(), Ox(), COS(), ALU.mult),
           r=eos[4:] + tabs + pbs, w=pbs, tag=f"{tag}t4")
        OP("dve", lambda e: e.tensor_tensor(dst_half_fn(1), dst_half_fn(1), TMP(), ALU.subtract),
           r=wr_bot + pbs, w=wr_bot, tag=f"{tag}bot2")

    def v_proj(j):
        xr = tuple(("xTbf", t) for t in range(DT)) + (("wbuf", 0), ("wbuf", 1))
        for q in range(4):
            kt = 4 * j + q
            for half in range(2):
                bk = 4 + half
                for t in range(DT):
                    OP("pe", lambda e, t=t, half=half, q=q, bk=bk:
                       e.matmul(pb[bk][:], xTbf[:, CW * t + 128 * q:CW * t + 128 * (q + 1)],
                                wbuf[:, 1024 * t + 512 * half:1024 * t + 512 * (half + 1)],
                                start=(t == 0), stop=False),
                       r=xr, w=(("pb", bk),), tag=f"vmm{kt}_{half}_{t}")
                OP("pe", lambda e, bk=bk, half=half:
                   e.matmul(pb[bk][:], onesrow[0:1, 0:128], bvrow[0:1, 512 * half:512 * (half + 1)],
                            start=False, stop=True),
                   r=("ones_row", "bvrow"), w=(("pb", bk),), tag=f"vb{kt}_{half}")
                OP("dve", lambda e, kt=kt, half=half, bk=bk:
                   e.tensor_copy(
                       V[:, VTW * kt + 520 * half:VTW * kt + 520 * (half + 1)]
                       .rearrange("p (h c) -> p h c", c=65)[:, :, 0:64],
                       pb[bk][:].rearrange("p (h c) -> p h c", c=64)),
                   r=(("pb", bk),), w=(("V", kt, half),), tag=f"vev{kt}_{half}")
            OP("dve", lambda e, kt=kt:
               e.memset(V[:, VTW * kt:VTW * kt + 1040]
                        .rearrange("p (h c) -> p h c", c=65)[:, :, 64:65], 1.0),
               w=(("Vones", kt),), tag=f"vones{kt}")

    def attention(j):
        nkt = 4 * j + 4

        def emit_scores(pr, kt):
            h0 = 2 * pr
            m = kt - 4 * j
            c0 = 128 * m if m >= 0 else 0
            s = kt % 2
            jj = kt // 4
            for hp in range(2):
                bk = 2 * s + hp
                wr = (("pb", bk),)
                if kt == nkt - 1 and hp == 1:
                    wr = wr + (("pace", j, pr),)
                OP("pe", lambda e, pr=pr, hp=hp, kt=kt, c0=c0, bk=bk:
                   e.matmul(pb[bk][:, c0:512],
                            KT[64 * hp:64 * (hp + 1), SEQ * pr + 128 * kt:SEQ * pr + 128 * (kt + 1)],
                            rot[64 * hp:64 * (hp + 1), CW * pr + c0:CW * (pr + 1)],
                            start=True, stop=True, tile_position=(64 * hp, 0)),
                   r=(("KT", pr, jj), "KTphys", ("xTbf", pr)), w=wr,
                   tag=f"sc{h0 + hp}_{kt}")

        def emit_ctx(pr, kt):
            h0 = 2 * pr
            m = kt - 4 * j
            c0 = 128 * m if m >= 0 else 0
            s = kt % 2
            for hp in range(2):
                off = 1024 * s + 512 * hp
                OP("pe", lambda e, kt=kt, h=h0 + hp, off=off, c0=c0, hp=hp:
                   e.matmul(pb[4 + hp][0:65, c0:512], V[:, VTW * kt + 65 * h:VTW * kt + 65 * (h + 1)],
                            eo[:, off + c0:off + 512],
                            start=(kt == 0), stop=(kt == nkt - 1)),
                   r=(("eo", 2 * s + hp), ("V", kt, (h0 + hp) // 8), ("Vones", kt), "Vphys"),
                   w=(("pb", 4 + hp),), tag=f"cx{h0 + hp}_{kt}")

        def norm_evac(pr):
            for hp in range(2):
                h = 2 * pr + hp
                OP("dve", lambda e, hp=hp: e.tensor_copy(pT[0:64, 512 * hp:512 * hp + 512],
                   pb[4 + hp][0:64, 0:512]),
                   r=(("pb", 4 + hp),), w=(("pT", hp),), tag=f"nu{h}")
                OP("dve", lambda e, hp=hp: e.tensor_copy(statrow[0:1, 512 * hp:512 * hp + 512],
                   pb[4 + hp][64:65, 0:512]),
                   r=(("pb", 4 + hp),), w=(("srow", hp),), tag=f"ns{h}")

        def norm_rest(pr):
            for hp in range(2):
                h = 2 * pr + hp
                t = pr
                OP("act", lambda e, hp=hp: e.activation(statrow[0:1, 512 * hp:512 * hp + 512],
                   statrow[0:1, 512 * hp:512 * hp + 512], AF.Ln),
                   r=(("srow", hp),), w=(("srow", hp),), tag=f"nl{h}")
                OP("act", lambda e, hp=hp: e.activation(statrow[0:1, 512 * hp:512 * hp + 512],
                   statrow[0:1, 512 * hp:512 * hp + 512], AF.Exp, scale=-1.0),
                   r=(("srow", hp),), w=(("srow", hp),), tag=f"nr{h}")
                OP("pe", lambda e, hp=hp: e.matmul(pb[6][0:64, 0:512], onesf32[0:1, 0:64],
                   statrow[0:1, 512 * hp:512 * hp + 512], start=True, stop=True),
                   r=("ones_f32", ("srow", hp)), w=(("pb", 6),), tag=f"nbc{h}")
                OP("dve", lambda e: e.tensor_copy(bcf[0:64, 0:512], pb[6][0:64, 0:512]),
                   r=(("pb", 6),), w=(("bcf", 0),), tag=f"nbe{h}")
                OP("dve", lambda e, t=t, hp=hp:
                   e.tensor_tensor(tab[64 * hp:64 * (hp + 1), CW * t:CW * (t + 1)],
                                   pT[0:64, 512 * hp:512 * hp + 512], bcf[0:64, 0:512], ALU.mult),
                   r=(("pT", hp), ("bcf", 0)), w=(("tab", t),), tag=f"nev{h}")

        def emit_exp_pair(pr, fkt):
            h0 = 2 * pr
            OP("act", lambda e: e.activation(eo[:, 0:2048], scA[:, 0:2048], AF.Exp, scale=0.125),
               r=(("pb", 0), ("pb", 1), ("pb", 2), ("pb", 3)),
               w=(("eo", 0), ("eo", 1), ("eo", 2), ("eo", 3)), tag=f"exp{h0}_{fkt}")

        def emit_exp_single(pr, kt):
            h0 = 2 * pr
            m = kt - 4 * j
            c0 = 128 * m if m >= 0 else 0
            s = kt % 2
            if m < 0:
                OP("act", lambda e, s=s:
                   e.activation(eo[:, 1024 * s:1024 * s + 1024], scA[:, 1024 * s:1024 * s + 1024],
                                AF.Exp, scale=0.125),
                   r=(("pb", 2 * s), ("pb", 2 * s + 1)),
                   w=(("eo", 2 * s), ("eo", 2 * s + 1)), tag=f"ex{h0}_{kt}")
            else:
                for hp in range(2):
                    off = 1024 * s + 512 * hp
                    OP("act", lambda e, off=off, c0=c0:
                       e.activation(eo[:, off + c0:off + 512], scA[:, off + c0:off + 512],
                                    AF.Exp, scale=0.125),
                       r=(("pb", 2 * s + hp),), w=(("eo", 2 * s + hp),), tag=f"ex{h0 + hp}_{kt}")
                    OP("dve", lambda e, off=off, c0=c0:
                       e.tensor_tensor(eo[:, off + c0:off + c0 + 128],
                                       eo[:, off + c0:off + c0 + 128], trimask[:], ALU.mult),
                       r=(("eo", 2 * s + hp), "trimask01"), w=(("eo", 2 * s + hp),),
                       tag=f"msk{h0 + hp}_{kt}")

        pending = None
        for pr in range(8):
            for kt in range(nkt):
                emit_scores(pr, kt)
                if kt == 1 and pending is not None:
                    pending()
                    pending = None
                emit_exp_single(pr, kt)
                if kt >= 1:
                    emit_ctx(pr, kt - 1)
            if j < DEV_CHUNKS - 1:
                s0_hi_tile(j + 1, pr)
            pc = pacers.pop((j, pr), None)
            if pc is not None:
                pc()
            if pending is not None:
                pending()
                pending = None
            emit_ctx(pr, nkt - 1)
            norm_evac(pr)
            pending = (lambda pr=pr: norm_rest(pr))
        pending()

    def out_proj(j):
        for dt in range(DT):
            bk = 4 + dt % 2
            for t in range(DT):
                OP("pe", lambda e, dt=dt, t=t, bk=bk:
                   e.matmul(pb[bk][:], wbuf[:, 1024 * t + 128 * dt:1024 * t + 128 * (dt + 1)],
                            tab[:, CW * t:CW * (t + 1)], start=(t == 0), stop=(t == DT - 1)),
                   r=(("tab", t), ("wbuf", 0), ("wbuf", 1)), w=(("pb", bk),), tag=f"omm{dt}_{t}")
            OP("dve", lambda e, dt=dt, bk=bk:
               e.scalar_tensor_tensor(xT32[:, CW * dt:CW * (dt + 1)], pb[bk][:],
                                      smalls[:, dt:dt + 1], xT32[:, CW * dt:CW * (dt + 1)],
                                      ALU.add, ALU.add),
               r=(("pb", bk), "bo", ("xT32", dt), "xT32phys"), w=(("xT32", dt),), tag=f"oev{dt}")

    def layer_norm(j, tag, double=False):
        for t in range(DT):
            par = t % 2
            hb = 512 * par
            sq = 1024 + 512 * par
            OP("dve", lambda e, t=t, hb=hb: e.tensor_copy(eo[:, hb:hb + 512],
               xT32[:, CW * t:CW * (t + 1)]),
               r=(("xT32", t), "xT32phys"), w=(("eo", par),), tag=f"{tag}hb{t}")
            OP("act", lambda e, hb=hb, sq=sq: e.activation(eo[:, sq:sq + 512],
               eo[:, hb:hb + 512], AF.Square),
               r=(("eo", par),), w=(("eo", 2 + par),), tag=f"{tag}sq{t}")
            OP("pe", lambda e, t=t, hb=hb: e.matmul(pb[4][0:1, 0:512], onescol[:], eo[:, hb:hb + 512],
               start=(t == 0), stop=(t == DT - 1)),
               r=("ones_col", ("eo", par)), w=(("pb", 4),), tag=f"{tag}s{t}")
            OP("pe", lambda e, t=t, sq=sq: e.matmul(pb[5][0:1, 0:512], onescol[:], eo[:, sq:sq + 512],
               start=(t == 0), stop=(t == DT - 1)),
               r=("ones_col", ("eo", 2 + par)), w=(("pb", 5),), tag=f"{tag}q{t}")
        OP("dve", lambda e: e.tensor_scalar(statrow[0:1, 0:512], pb[4][0:1, 0:512],
           1.0 / D, None, ALU.mult), r=(("pb", 4),), w=(("srow", 0),), tag=f"{tag}mu")
        OP("dve", lambda e: e.tensor_tensor(statrow[0:1, 512:1024], statrow[0:1, 0:512],
           statrow[0:1, 0:512], ALU.mult), r=(("srow", 0),), w=(("srow", 1),), tag=f"{tag}m2")
        OP("dve", lambda e: e.scalar_tensor_tensor(statrow[0:1, 512:1024], pb[5][0:1, 0:512],
           1.0 / D, statrow[0:1, 512:1024], ALU.mult, ALU.subtract),
           r=(("pb", 5), ("srow", 1)), w=(("srow", 1),), tag=f"{tag}var")
        OP("act", lambda e: e.activation(statrow[0:1, 512:1024], statrow[0:1, 512:1024],
           AF.Ln, bias=smalls[0:1, 17:18]), r=(("srow", 1), "epsc"), w=(("srow", 1),), tag=f"{tag}ln")
        if double:
            OP("act", lambda e: e.activation(bcf[0:1, 0:512], statrow[0:1, 512:1024],
               AF.Exp, scale=-1.0), r=(("srow", 1),), w=(("bcf", 0),), tag=f"{tag}dv1")
            OP("dve", lambda e: e.tensor_scalar(bcf[0:1, 0:512], bcf[0:1, 0:512],
               -LN_EPS, 1.0, ALU.mult, ALU.add), r=(("bcf", 0),), w=(("bcf", 0),),
               tag=f"{tag}dv2")
            OP("act", lambda e: e.activation(bcf[0:1, 0:512], bcf[0:1, 0:512], AF.Ln,
               bias=smalls[0:1, 17:18]), r=(("bcf", 0), "epsc"), w=(("bcf", 0),),
               tag=f"{tag}dv3")
            OP("dve", lambda e: e.tensor_tensor(statrow[0:1, 512:1024], statrow[0:1, 512:1024],
               bcf[0:1, 0:512], ALU.add), r=(("srow", 1), ("bcf", 0)), w=(("srow", 1),),
               tag=f"{tag}dv4")
        OP("act", lambda e: e.activation(statrow[0:1, 512:1024], statrow[0:1, 512:1024],
           AF.Exp, scale=-0.5), r=(("srow", 1),), w=(("srow", 1),), tag=f"{tag}rstd")
        OP("pe", lambda e: e.matmul(pb[4][:, 0:512], onesf32[0:1, :], statrow[0:1, 512:1024],
           start=True, stop=True), r=("ones_f32", ("srow", 1)), w=(("pb", 4),), tag=f"{tag}bc1")
        OP("dve", lambda e: e.tensor_tensor(statrow[0:1, 0:512], statrow[0:1, 0:512],
           statrow[0:1, 512:1024], ALU.mult), r=(("srow", 0), ("srow", 1)), w=(("srow", 0),), tag=f"{tag}mrs")
        OP("pe", lambda e: e.matmul(pb[5][:, 0:512], onesf32[0:1, :], statrow[0:1, 0:512],
           start=True, stop=True), r=("ones_f32", ("srow", 0)), w=(("pb", 5),), tag=f"{tag}bc2")
        OP("dve", lambda e: e.tensor_copy(bcf[:, 0:512], pb[4][:, 0:512]),
           r=(("pb", 4),), w=(("bcf", 0),), tag=f"{tag}be1")
        OP("dve", lambda e: e.tensor_copy(bcf[:, 512:1024], pb[5][:, 0:512]),
           r=(("pb", 5),), w=(("bcf", 1),), tag=f"{tag}be2")
        for t in range(DT):
            OP("dve", lambda e, t=t: e.tensor_tensor(xT32[:, CW * t:CW * (t + 1)],
               xT32[:, CW * t:CW * (t + 1)], bcf[:, 0:512], ALU.mult),
               r=(("xT32", t), ("bcf", 0), "xT32phys"), w=(("xT32", t),), tag=f"{tag}n1{t}")
            OP("dve", lambda e, t=t: e.tensor_tensor(xT32[:, CW * t:CW * (t + 1)],
               xT32[:, CW * t:CW * (t + 1)], bcf[:, 512:1024], ALU.subtract),
               r=(("xT32", t), ("bcf", 1), "xT32phys"), w=(("xT32", t),), tag=f"{tag}n2{t}")

    def finalize(j):
        for q in range(4):
            qt = 4 * j + q
            for tg4 in range(2):
                for tt4 in range(4):
                    t = 4 * tg4 + tt4
                    OP("pe", lambda e, t=t, q=q, tt4=tt4: e.transpose(
                       pb[6][:, 128 * tt4:128 * (tt4 + 1)],
                       xT32[:, CW * t + 128 * q:CW * t + 128 * (q + 1)], ident32[:]),
                       r=(("xT32", t), "xT32phys", "ident_f32"), w=(("pb", 6),), tag=f"ftr{qt}_{t}")
                OP("dve", lambda e, tg4=tg4: e.tensor_copy(
                   xnat[:, 512 * tg4:512 * (tg4 + 1)], pb[6][:, 0:512]),
                   r=(("pb", 6),), w=(("xnat", 0),), tag=f"fev{qt}_{tg4}")
            if j >= 4:
                OP("pool", lambda e, qt=qt: e.dma_start(
                   out_d[128 * qt:128 * (qt + 1), :], xnat[:, :]),
                   r=(("xnat", 0),), w=(("outrow", qt),), dma="ost", tag=f"ost{qt}")
            else:
                OP("pool", lambda e, qt=qt: e.dma_start(
                   h1nat_d[128 * qt:128 * (qt + 1), :], xnat[:, :]),
                   r=(("xnat", 0),), w=(("h1row", qt),), dma="h1st", tag=f"hst{qt}")
                OP("dve", lambda e: e.tensor_copy(eo[:, 0:1024], xnat[:, :]),
                   r=(("xnat", 0),), w=(("eo", 0), ("eo", 1)), tag=f"h1c{qt}")
                OP("pool", lambda e, qt=qt: e.dma_start(
                   h1bf_d[128 * qt:128 * (qt + 1), :], eo[:, 0:1024]),
                   r=(("eo", 0), ("eo", 1)), w=(("h1bfr", qt),), dma="h1bst", tag=f"hbs{qt}")

    pacers = {}
    for j in range(DEV_CHUNKS):
        if j == 1:
            routing()
        if j in (4, 5, 6):
            wn = {4: "gug", 5: "guu", 6: "dn"}[j]
            RSL = NE * D // 8
            for k in range(8):
                def mk(wn=wn, k=k, j=j, RSL=RSL):
                    def em():
                        OP("pool", lambda e, wn=wn, k=k, RSL=RSL: e.dma_start(
                            w_bf[wn][RSL * k:RSL * (k + 1), :],
                            io[wn][RSL * k:RSL * (k + 1), :]),
                           r=(("pace", j, k),), w=((wn + "_bf", k),), dma="wcastf",
                           tag=f"cast_{wn}{k}")
                    return em
                pacers[(j, k)] = mk()

        if j == 0:
            for t in range(DT):
                s0_hi_tile(0, t)
        load_w_pair("wk_e", "wk_o", f"wk{j}")
        load_tables(j)
        proj_eo(j, 32, f"k{j}")
        rope_to(j,
                lambda h, j=j: KT[:].rearrange("p (d s) -> p d s", s=SEQ)
                [:, 4 * h:4 * (h + 1), CW * j:CW * (j + 1)],
                lambda d, j=j: ("KT", d, j), f"kr{j}")
        s0_hi_to32(j)
        load_w_full("wv", f"wv{j}")
        v_proj(j)
        load_w_pair("wq_e", "wq_o", f"wq{j}")
        proj_eo(j, 24, f"q{j}")
        rope_to(j,
                lambda h: rot[:, 2048 * h:2048 * (h + 1)].rearrange("p (d s) -> p d s", s=512),
                lambda d: ("xTbf", d), f"qr{j}")
        s0_lo(j)
        attention(j)
        load_w_full("wo", f"wo{j}")
        out_proj(j)
        if j >= 4:
            layer_norm(j, f"lnd_{j}", double=True)
        else:
            layer_norm(j, f"ln1_{j}")
        finalize(j)
        if debug and j == 0:
            dbg_t("rot0", [128, 4096], BF16)
            OP("sp", lambda e: e.dma_start(dbg["rot0"][:], rot[:]),
               r=tuple(("xTbf", d) for d in range(8)), dma="dbg_rot0", tag="dump_rot0")
            dbg_t("ctxN0", [128, 4096], BF16)
            OP("sp", lambda e: e.dma_start(dbg["ctxN0"][:], tab[:]),
               r=tuple(("tab", t) for t in range(8)), dma="dbg_ctx0", tag="dump_ctx0")

    if debug:
        dbg_t("KT", [128, DT * SEQ], BF16)
        OP("sp", lambda e: e.dma_start(dbg["KT"][:], KT[:]),
           r=tuple(("KT", t, jj) for t in range(8) for jj in range(8)) + ("KTphys",),
           dma="dbg_KT", tag="dump_KT")
        dbg_t("V", [128, 32 * VTW], BF16)
        OP("sp", lambda e: e.dma_start(dbg["V"][:], V[:]),
           r=tuple(("V", kt, hf) for kt in range(32) for hf in range(2)) +
             tuple(("Vones", kt) for kt in range(32)) + ("Vphys",),
           dma="dbg_V", tag="dump_V")
        dbg_t("h1", [SEQ // 2, D], F32)
        OP("sp", lambda e: e.dma_start(dbg["h1"][:], h1nat_d[:]),
           r=tuple(("h1row", qt) for qt in range(16)), dma="dbg_h1", tag="dump_h1")

    if DEV_CHUNKS < 2:
        routing()
    if DEV_CHUNKS < 7:
        for wn in ("gug", "guu", "dn")[max(0, DEV_CHUNKS - 4):]:
            OP("pool", lambda e, s=io[wn], d=w_bf[wn]: e.dma_start(d[:], s[:]),
               w=tuple((wn + "_bf", k) for k in range(8)), dma="wcastf", tag=f"cast_{wn}")
    if not DEV_FFN:
        done = []
        for j in range(DEV_CHUNKS):
            for q in range(4):
                qt = 4 * j + q
                done.append(("outrow", qt) if j >= 4 else ("h1row", qt))
        OP("sp", lambda e: e.nop(), r=tuple(done), tag="fin_dev")
        p.emit(nc, es)
        nc._es = es
        return nc, sorted(dbg.keys())
    OP("pe", lambda e: e.matmul(pb[6][0:1, 0:1], onescol[0:1, 0:1], onescol[0:1, 0:1],
       start=True, stop=True),
       r=("ones_col",),
       w=("KTphys", "xT32phys", "Vphys", "ffn_gate", ("pb", 6), ("pT", 0), ("pT", 1)) +
         tuple(("tab", t) for t in range(8)) + tuple(("eo", i) for i in range(8)),
       tag="ffn_gate")

    OP("act", load_set(SET_SIG), tag="tab_sig")

    def ffn_w_load(nm, erow, tag, par):
        if par == 2:
            OP("sp", lambda e, nm=nm: e.dma_start(
                tab[:, 0:4096].rearrange("p (t c) -> p t c", c=1024),
                w_bf[nm][1024 * erow:1024 * erow + 512, :]
                .rearrange("(t p) c -> p t c", p=128)),
               r=("ffn_gate",) + tuple((nm + "_bf", k) for k in range(8)),
               w=(("fw2", 0),), dma="ffw2", tag=tag + "a")
            OP("sp", lambda e, nm=nm: e.dma_start(
                eo[:, 0:4096].rearrange("p (t c) -> p t c", c=1024),
                w_bf[nm][1024 * erow + 512:1024 * (erow + 1), :]
                .rearrange("(t p) c -> p t c", p=128)),
               r=("ffn_gate",) + tuple((nm + "_bf", k) for k in range(8)),
               w=(("fw2", 1),), dma="ffw2", tag=tag + "b")
            return
        dst = wbuf[:, :] if par == 0 else wbuf2
        wr = ((f"fw{par}", 0),) if par == 1 else ((f"fw{par}", 0), ("wbuf", 0), ("wbuf", 1))
        OP("sp", lambda e, dst=dst, nm=nm: e.dma_start(
            dst.rearrange("p (t c) -> p t c", c=1024),
            w_bf[nm][1024 * erow:1024 * (erow + 1), :]
            .rearrange("(t p) c -> p t c", p=128)),
           r=("ffn_gate",) + tuple((nm + "_bf", k) for k in range(8)),
           w=wr, dma=f"ffw{par}", tag=tag)

    SC = ((0, 512), (512, 128))

    def ffn_dispatch(ex):
        par = ex % 2
        h1r = tuple(("h1bfr", qt) for qt in range(16))
        for kg in range(5):
            kcol = 5 * ex + kg
            OP("pool", lambda e, kcol=kcol: e.indirect_dma_start(
                xgbf[:, :], None, h1bf_d[:, :], IOA(ap=moei[:, kcol:kcol + 1], axis=0)),
               r=(("itc", kcol), "ffn_gate", ("xgbf",)) + h1r,
               w=(("xgbf",),), dma="gx", tag=f"gx{ex}_{kg}")
            OP("act", lambda e, par=par, kg=kg: e.dma_start(
                xg_d[CAP * par + 128 * kg:CAP * par + 128 * (kg + 1), :], xgbf[:, :]),
               r=(("xgbf",),), w=(("xgd", par, kg),), dma=f"gxs{par}", tag=f"gw{ex}_{kg}")
        for t in range(DT):
            OP("sp", lambda e, t=t, par=par: e.dma_start(
               xeT_ab[par][:, CAP * t:CAP * (t + 1)],
               xg_d[CAP * par:CAP * (par + 1), 128 * t:128 * (t + 1)], transpose=True),
               r=tuple(("xgd", par, kg) for kg in range(5)) + ("ffn_gate",) +
                 tuple(("xeT", par, tt) for tt in range(DT)),
               w=(("xeT", par, t),), dma=f"xeTT{par}", tag=f"xt{ex}_{t}")

    def w3sl(t, it):
        if t < 4:
            return tab[:, 1024 * t + 128 * it:1024 * t + 128 * (it + 1)]
        return eo[:, 1024 * (t - 4) + 128 * it:1024 * (t - 4) + 128 * (it + 1)]

    def ffn_expert(ex):
        xpar = ex % 2
        xeT = xeT_ab[xpar]
        wpar = [0, 1, 2]
        wb = [wbuf[:, :], wbuf2, None]
        rd_xe0 = tuple(("xeT", xpar, kg) for kg in range(DT))
        ffn_w_load("gug", ex, f"wg{ex}", wpar[0])
        pend_gl = None
        for it in range(DT):
            for sci, (s0, swd) in enumerate(SC):
                bk = (2 * it + sci) % 6
                sl = (2 * it + sci) % 2
                g1v = g1s[:, 0:swd] if sl == 0 else sgs[:, 0:swd]
                sgv = pT[:, 512 * sl:512 * sl + swd]
                for t in range(DT):
                    OP("pe", lambda e, it=it, t=t, s0=s0, swd=swd, bk=bk:
                       e.matmul(pb[bk][:, 0:swd], wb[0][:, 1024 * t + 128 * it:1024 * t + 128 * (it + 1)],
                                xeT[:, CAP * t + s0:CAP * t + s0 + swd],
                                start=(t == 0), stop=(t == DT - 1)),
                       r=rd_xe0 + ((f"fw{wpar[0]}", 0),), w=(("pb", bk),), tag=f"gm{ex}_{it}_{sci}_{t}")
                OP("dve", lambda e, it=it, swd=swd, ex=ex, bk=bk, g1v=g1v:
                   e.tensor_scalar(g1v, pb[bk][:, 0:swd],
                                   gub_sb[:, 8 * ex + it:8 * ex + it + 1], 7.0, ALU.add, ALU.min),
                   r=(("pb", bk), "gubg", "ffn_gate"), w=(("g1s", sl),), tag=f"gg{ex}_{it}_{swd}")
                OP("act", lambda e, swd=swd, g1v=g1v, sgv=sgv: e.activation(sgv, g1v,
                   AF.Sigmoid, scale=1.702),
                   r=(("g1s", sl), "ffn_gate"), w=(("sgs", sl),), tag=f"gs{ex}_{it}_{swd}")
                if pend_gl is not None:
                    pend_gl()
                pend_gl = (lambda it=it, s0=s0, swd=swd, sl=sl, g1v=g1v, sgv=sgv:
                   OP("dve", lambda e, it=it, s0=s0, swd=swd, g1v=g1v, sgv=sgv:
                      e.tensor_tensor(glu_all[:, CAP * it + s0:CAP * it + s0 + swd],
                                      g1v, sgv, ALU.mult),
                      r=(("g1s", sl), ("sgs", sl), "ffn_gate"), w=(("ydat", it),),
                      tag=f"gl{ex}_{it}_{swd}"))
        pend_gl()
        ffn_w_load("guu", ex, f"wu{ex}", wpar[1])
        for it in range(DT):
            for sci, (s0, swd) in enumerate(SC):
                bk = (2 * it + sci) % 6
                for t in range(DT):
                    OP("pe", lambda e, it=it, t=t, s0=s0, swd=swd, bk=bk:
                       e.matmul(pb[bk][:, 0:swd], wb[1][:, 1024 * t + 128 * it:1024 * t + 128 * (it + 1)],
                                xeT[:, CAP * t + s0:CAP * t + s0 + swd],
                                start=(t == 0), stop=(t == DT - 1)),
                       r=rd_xe0 + ((f"fw{wpar[1]}", 0),), w=(("pb", bk),), tag=f"um{ex}_{it}_{sci}_{t}")
                OP("dve", lambda e, it=it, swd=swd, ex=ex, bk=bk:
                   e.tensor_scalar(u1s[:, 0:swd], pb[bk][:, 0:swd],
                                   gub_sb[:, 64 + 8 * ex + it:64 + 8 * ex + it + 1], 7.0, ALU.add, ALU.min),
                   r=(("pb", bk), "gubu", "ffn_gate"), w=(("u1s",),), tag=f"uu{ex}_{it}_{swd}")
                OP("dve", lambda e, swd=swd: e.tensor_scalar(u1s[:, 0:swd], u1s[:, 0:swd],
                   -7.0, None, ALU.max),
                   r=(("u1s",),), w=(("u1s",),), tag=f"uc{ex}_{it}_{swd}")
                OP("dve", lambda e, it=it, s0=s0, swd=swd:
                   e.scalar_tensor_tensor(intT[:, CAP * it + s0:CAP * it + s0 + swd],
                                          u1s[:, 0:swd], 1.0,
                                          glu_all[:, CAP * it + s0:CAP * it + s0 + swd],
                                          ALU.add, ALU.mult),
                   r=(("u1s",), ("ydat", it), "ffn_gate"), w=(("intT", it),), tag=f"ui{ex}_{it}_{swd}")
        ffn_w_load("dn", ex, f"wd{ex}", wpar[2])
        rd_int = tuple(("intT", it) for it in range(DT)) + (("fw2", 0), ("fw2", 1))
        for dt in range(DT):
            for sci, (s0, swd) in enumerate(SC):
                bk = (2 * dt + sci) % 6
                for t in range(DT):
                    OP("pe", lambda e, dt=dt, t=t, s0=s0, swd=swd, bk=bk:
                       e.matmul(pb[bk][:, 0:swd], w3sl(t, dt),
                                intT[:, CAP * t + s0:CAP * t + s0 + swd],
                                start=(t == 0), stop=(t == DT - 1)),
                       r=rd_int, w=(("pb", bk),), tag=f"dm{ex}_{dt}_{sci}_{t}")
                OP("dve", lambda e, dt=dt, s0=s0, swd=swd, ex=ex, bk=bk:
                   e.tensor_scalar(yTbf[:, CAP * dt + s0:CAP * dt + s0 + swd], pb[bk][:, 0:swd],
                                   gub_sb[:, 128 + 8 * ex + dt:128 + 8 * ex + dt + 1], None, ALU.add),
                   r=(("pb", bk), "dnbc", "ffn_gate"), w=(("ydat", dt),), tag=f"de{ex}_{dt}_{swd}")

    def ffn_y(ex):
        par = ex % 2
        for t in range(DT):
            OP("act", lambda e, t=t, par=par: e.dma_start(
               yt_d[D * par + 128 * t:D * par + 128 * (t + 1), :],
               yTbf[:, CAP * t:CAP * (t + 1)]),
               r=(("ydat", t),), w=(("ytd", par, t),), dma=f"yst{par}", tag=f"yw{ex}_{t}")
        for kg in range(5):
            kcol = 5 * ex + kg
            yp = kg % 2
            yn = ynat if yp == 0 else ynat2
            OP("sp", lambda e, kg=kg, par=par, yn=yn: e.dma_start(
               yn[:, :], yt_d[D * par:D * (par + 1), 128 * kg:128 * (kg + 1)], transpose=True),
               r=tuple(("ytd", par, t) for t in range(DT)) + (("ynat", yp),),
               w=(("ynat", yp),), dma=f"ynT{yp}", tag=f"yr{ex}_{kg}")
            OP("dve", lambda e, kcol=kcol, yn=yn: e.tensor_scalar(yn[:, :], yn[:, :],
               coefc[:, kcol:kcol + 1], None, ALU.mult),
               r=(("ynat", yp), ("cfc", kcol), "ffn_gate"), w=(("ynat", yp),), tag=f"yc{ex}_{kg}")
            OP("pool", lambda e, kcol=kcol, yn=yn: e.indirect_dma_start(
                pairbuf_d[:, :], IOA(ap=moei[:, 40 + kcol:41 + kcol], axis=0), yn[:, :], None),
               r=(("ynat", yp), ("ipc", kcol)) +
                 tuple(("pairbuf", t, hh) for t in range(PB_ROWS // 128) for hh in range(2)),
               w=(("pbscat", ex, kg), "pbphys"), dma="sc_pb", tag=f"ys{ex}_{kg}")

    ffn_dispatch(0)
    for ex in range(DEV_NEXPERTS):
        ffn_expert(ex)
        if ex + 1 < DEV_NEXPERTS:
            ffn_dispatch(ex + 1)
        ffn_y(ex)

    OP("act", load_set(SET_LNEXP), tag="tab_lnexp1")
    pb2v = pairbuf_d[:, :].rearrange("(a b) c -> a (b c)", b=2)
    rd_scat = tuple(("pbscat", ex, kg) for ex in range(NE) for kg in range(5)) + \
        tuple(("pairbuf", t, hh) for t in range(PB_ROWS // 128) for hh in range(2))
    for tg in range(16):
        pr = tg % 2
        pbts = pbt if pr == 0 else pbt2
        xgs = xgp[pr]
        cmb = cmbp[pr]
        ffns = eo[:, 1024 * pr:1024 * (pr + 1)]
        sqs = eo[:, 2048 + 1024 * pr:2048 + 1024 * (pr + 1)]
        sc0 = 20 if pr == 0 else 8
        OP("sp", lambda e, tg=tg, pbts=pbts: e.dma_start(pbts[:, :], pb2v[128 * tg:128 * (tg + 1), :]),
           r=rd_scat + ("ffn_gate", ("pbt", pr)), w=(("pbt", pr),), dma=f"pbl{pr}", tag=f"cl{tg}")
        OP("sp", lambda e, tg=tg, xgs=xgs: e.dma_start(xgs, h1nat_d[128 * tg:128 * (tg + 1), :]),
           r=tuple(("h1row", qt) for qt in range(16)) + (("cxg", pr), "ffn_gate"),
           w=(("cxg", pr),), dma=f"h1l{pr}", tag=f"ch{tg}")
        OP("dve", lambda e, pbts=pbts, ffns=ffns: e.tensor_tensor(
           ffns, pbts[:, 0:1024], pbts[:, 1024:2048], ALU.add),
           r=(("pbt", pr), "ffn_gate"), w=(("cffn", pr),), tag=f"cf{tg}")
        OP("dve", lambda e, cmb=cmb, xgs=xgs, ffns=ffns: e.tensor_tensor(
           cmb, xgs, ffns, ALU.add),
           r=(("cxg", pr), ("cffn", pr)), w=(("cmb", pr),), tag=f"ct{tg}")
        OP("dve", lambda e, cmb=cmb, sc0=sc0: e.reduce_sum(smalls[:, sc0:sc0 + 1], cmb, axis=AX.X),
           r=(("cmb", pr),), w=(("cs", pr, 0),), tag=f"cs{tg}")
        OP("act", lambda e, cmb=cmb, sqs=sqs: e.activation(sqs, cmb, AF.Square),
           r=(("cmb", pr),), w=(("csq", pr),), tag=f"cqs{tg}")
        OP("dve", lambda e, sqs=sqs, sc0=sc0: e.reduce_sum(smalls[:, sc0 + 1:sc0 + 2], sqs, axis=AX.X),
           r=(("csq", pr),), w=(("cs", pr, 1),), tag=f"cq{tg}")
        OP("dve", lambda e, sc0=sc0: e.tensor_scalar(smalls[:, sc0:sc0 + 1], smalls[:, sc0:sc0 + 1],
           1.0 / D, None, ALU.mult), r=(("cs", pr, 0),), w=(("cs", pr, 0),), tag=f"cm{tg}")
        OP("dve", lambda e, sc0=sc0: e.tensor_tensor(smalls[:, sc0 + 2:sc0 + 3],
           smalls[:, sc0:sc0 + 1], smalls[:, sc0:sc0 + 1], ALU.mult),
           r=(("cs", pr, 0),), w=(("cs", pr, 2),), tag=f"c2{tg}")
        OP("dve", lambda e, sc0=sc0: e.scalar_tensor_tensor(smalls[:, sc0 + 1:sc0 + 2],
           smalls[:, sc0 + 1:sc0 + 2], 1.0 / D, smalls[:, sc0 + 2:sc0 + 3],
           ALU.mult, ALU.subtract),
           r=(("cs", pr, 1), ("cs", pr, 2)), w=(("cs", pr, 1),), tag=f"cv{tg}")
        OP("act", lambda e, sc0=sc0: e.activation(smalls[:, sc0 + 3:sc0 + 4],
           smalls[:, sc0 + 1:sc0 + 2], AF.Ln, bias=smalls[:, 17:18]),
           r=(("cs", pr, 1), "epsc"), w=(("cs", pr, 3),), tag=f"cl2{tg}")
        OP("act", lambda e, sc0=sc0: e.activation(smalls[:, sc0 + 3:sc0 + 4],
           smalls[:, sc0 + 3:sc0 + 4], AF.Exp, scale=-0.5),
           r=(("cs", pr, 3),), w=(("cs", pr, 3),), tag=f"cr{tg}")
        OP("dve", lambda e, cmb=cmb, sc0=sc0: e.tensor_scalar(cmb, cmb, smalls[:, sc0:sc0 + 1],
           smalls[:, sc0 + 3:sc0 + 4], ALU.subtract, ALU.mult),
           r=(("cmb", pr), ("cs", pr, 0), ("cs", pr, 3)), w=(("cmb", pr),), tag=f"cn{tg}")
        OP("sp", lambda e, tg=tg, cmb=cmb: e.dma_start(out_d[128 * tg:128 * (tg + 1), :], cmb),
           r=(("cmb", pr),), w=(("outrow", tg),), dma=f"oc{pr}", tag=f"cst{tg}")

    OP("sp", lambda e: e.nop(), r=tuple(("outrow", qt) for qt in range(32)), tag="fin")

    p.emit(nc, es)
    nc._es = es
    return nc, sorted(dbg.keys())


def _prep_inputs(L0_attn_x, L0_attn_Wq, L0_attn_bq, L0_attn_Wk, L0_attn_bk,
                 L0_attn_Wv, L0_attn_bv, L0_attn_Wo, L0_attn_bo,
                 L0_ffn_router_indices, L0_ffn_routing_weights,
                 L0_ffn_gate_up_proj, L0_ffn_gate_up_proj_bias,
                 L0_ffn_down_proj, L0_ffn_down_proj_bias):
    consts = host_constants()
    f32 = np.float32
    asf = np.ascontiguousarray
    Wq = np.asarray(L0_attn_Wq, f32); Wk = np.asarray(L0_attn_Wk, f32)
    Wv = np.asarray(L0_attn_Wv, f32); Wo = np.asarray(L0_attn_Wo, f32)
    bq = np.asarray(L0_attn_bq, f32); bk = np.asarray(L0_attn_bk, f32)
    bv = np.asarray(L0_attn_bv, f32); bo = np.asarray(L0_attn_bo, f32)
    ridx = np.asarray(L0_ffn_router_indices, np.int32)
    rw = np.asarray(L0_ffn_routing_weights, f32)
    gup = np.asarray(L0_ffn_gate_up_proj, f32)
    gupb = np.asarray(L0_ffn_gate_up_proj_bias, f32)
    dnp = np.asarray(L0_ffn_down_proj, f32)
    dnpb = np.asarray(L0_ffn_down_proj_bias, f32)
    x = np.asarray(L0_attn_x, f32)

    shared = {
        "wq_e": asf(Wq[:, 0::2]), "wq_o": asf(Wq[:, 1::2]),
        "wk_e": asf(Wk[:, 0::2]), "wk_o": asf(Wk[:, 1::2]),
        "wv": Wv, "wo": Wo,
        "bq_e": asf(bq[0::2].reshape(4, 128).T), "bq_o": asf(bq[1::2].reshape(4, 128).T),
        "bk_e": asf(bk[0::2].reshape(4, 128).T), "bk_o": asf(bk[1::2].reshape(4, 128).T),
        "bv": asf(bv[None, :]),
        "bo": asf(bo.reshape(8, 128).T),
        "gug": asf(gup[:, :, 0::2].reshape(NE * D, D)),
        "guu": asf(gup[:, :, 1::2].reshape(NE * D, D)),
        "dn": asf(dnp.reshape(NE * D, D)),
        "gub_g": asf(gupb[:, 0::2].reshape(NE, 8, 128).transpose(2, 0, 1).reshape(128, 64)),
        "gub_u": asf(gupb[:, 1::2].reshape(NE, 8, 128).transpose(2, 0, 1).reshape(128, 64)),
        "dnb": asf(dnpb.reshape(NE, 8, 128).transpose(2, 0, 1).reshape(128, 64)),
        "ridx_pair": asf(ridx.reshape(-1).reshape(128, 32)),
        "rw_bp": asf(np.repeat(rw, 2, axis=0).reshape(128, 256)),
    }
    shared.update(consts)
    in_maps = []
    for c in range(B):
        m = dict(shared)
        m["x"] = asf(x[c])
        m["maskcol"] = np.full((128, 1), 1.0 if c == 0 else 0.0, f32)
        in_maps.append(m)
    return in_maps


def kernel(**inputs):
    in_maps = _prep_inputs(**inputs)
    key = ("prog", DEBUG, DEV_CHUNKS, DEV_FFN, DEV_NEXPERTS, DEV_COMBINE, DEV_FFN_STAGE, DEV_DUMP_ROUTING)
    if key not in _NC_CACHE:
        _NC_CACHE[key] = build_program(debug=DEBUG)
    nc, _ = _NC_CACHE[key]
    res = run_bass_kernel_spmd(nc, in_maps, core_ids=list(range(B)))
    out = np.stack([np.asarray(r["out"]) for r in res.results], axis=0)
    return out.astype(np.float32)
